# version 04
# baseline (speedup 1.0000x reference)
import os
import numpy as np
from contextlib import ExitStack

import concourse.bass as bass
import concourse.mybir as mybir
from concourse.bass_utils import run_bass_kernel_spmd

F32 = mybir.dt.float32
BF16 = mybir.dt.bfloat16
AF = mybir.ActivationFunctionType
ALU = mybir.AluOpType
AXX = mybir.AxisListType.X

B, N, D, H, HD, C = 4, 4096, 1024, 16, 64, 128
T = 2048
NT = 16
NCH = 32
E, I = 16, 1024
RS = 2.826
DEBUG = False

ENGS = ("tensor", "vector", "scalar", "gpsimd", "sync")


class Plan:
    def __init__(self, nc, ctx):
        self.nc = nc
        self.ctx = ctx
        self.items = {e: [] for e in ENGS}
        self.sems = {}
        self.totals = {}
        ser = os.environ.get("KB_SERIAL", "")
        if ser:
            lo, hi = ser.split(":")
            self.ser_lo, self.ser_hi = int(lo), int(hi)
        else:
            self.ser_lo, self.ser_hi = 0, -1
        self.eidx = 0
        self.last = None
        self.main = {}
        self.pure = set()
        self.nopf = {}

    def sem(self, name):
        h = self.ctx.enter_context(self.nc.semaphore(name))
        self.sems[name] = h
        self.totals[name] = 0
        return name

    def emit(self, eng, fn, waits=(), inc=None, inc_n=1, extra=()):
        if extra:
            assert inc_n == 1
            cands = ([inc] if inc is not None else []) + list(extra)
            pures = [s for s in cands if s in self.pure]
            carried = pures[0] if pures else self.main[eng].inc()
            others = [s for s in cands if s != carried]
            self.emit(eng, fn, waits=waits, inc=carried)
            cval = self.totals[carried]
            for s in others:
                self.emit(eng, self.nopf[eng], waits=[(carried, cval)], inc=s)
            return
        i = self.eidx
        self.eidx += 1
        ws = [(s, v) for (s, v) in waits if v > 0]
        if self.ser_lo <= i < self.ser_hi:
            if self.last is not None:
                lsem, lval, leng, ldma = self.last
                if (leng != eng or ldma) and lval > 0:
                    ws.append((lsem, lval))
            if inc is None and not extra and eng in self.main:
                inc = self.main[eng].inc()
        for s, v in ws:
            assert v <= self.totals[s], f"wait {s}>={v} exceeds planned {self.totals[s]}"
        if inc is not None:
            self.totals[inc] += inc_n
            self.last = (inc, self.totals[inc], eng, inc_n == 16)
        for s in extra:
            self.totals[s] += 1
            self.last = (s, self.totals[s], eng, False)
        self.items[eng].append((fn, ws, inc, inc_n, tuple(extra)))

    def run(self, eng_name, eng):
        for fn, waits, inc, inc_n, extra in self.items[eng_name]:
            for (s, v) in waits:
                eng.wait_ge(self.sems[s], v)
            ins = fn(eng)
            if inc is not None or extra:
                assert ins is not None
            if inc is not None:
                ins = ins.then_inc(self.sems[inc], inc_n)
            for s in extra:
                ins = ins.then_inc(self.sems[s], 1)


class Ctr:
    def __init__(self, p, name):
        self.name = p.sem(name)
        p.pure.add(self.name)
        self.n = 0

    def inc(self, k=1):
        self.n += k
        return self.name

    @property
    def at(self):
        return (self.name, self.n)


class Ring:

    def __init__(self, p, name, slots, fill=False):
        self.slots = slots
        self.sem = [p.sem(f"{name}{i}") for i in range(len(slots))]
        self.rel = [0] * len(slots)
        self.i = 0
        if fill:
            self.fsem = [p.sem(f"{name}f{i}") for i in range(len(slots))]
            self.fcnt = [0] * len(slots)

    def acquire(self, rel_units=1):
        i = self.i % len(self.slots)
        self.i += 1
        w = [(self.sem[i], self.rel[i])]
        self.rel[i] += rel_units
        return self.slots[i], self.sem[i], w, self.rel[i]

    def fill_tok(self):
        i = (self.i - 1) % len(self.slots)
        self.fcnt[i] += 16
        return self.fsem[i], self.fcnt[i]


def build_nc(debug=DEBUG, level=None):
    if level is None:
        level = int(os.environ.get("KBUILD_LEVEL", "99"))
    sub = int(os.environ.get("KB_SUB", "5"))
    nqk = int(os.environ.get("KB_NQK", "2"))
    nc = bass.Bass("TRN2", target_bir_lowering=False,
                   dynamic_dma_scratch_size=16384,
                   detect_race_conditions=False)

    dt_i = nc.dram_tensor
    x_own = dt_i("x_own", [T, D], F32, kind="ExternalInput")
    x_pre = dt_i("x_pre", [T, D], F32, kind="ExternalInput")
    wqT = dt_i("wqT", [D, D], F32, kind="ExternalInput")
    wkT = dt_i("wkT", [D, D], F32, kind="ExternalInput")
    wvT = dt_i("wvT", [D, D], F32, kind="ExternalInput")
    woT = dt_i("woT", [D, D], F32, kind="ExternalInput")
    wfT = dt_i("wfT", [D, H], F32, kind="ExternalInput")
    delta = dt_i("delta", [H, 1], F32, kind="ExternalInput")
    nw2 = dt_i("nw2", [128, 8], F32, kind="ExternalInput")
    gwT = dt_i("gwT", [D, E], F32, kind="ExternalInput")
    ebias = dt_i("ebias", [1, E], F32, kind="ExternalInput")
    WG = dt_i("WG", [E, 8, D, 128], F32, kind="ExternalInput")
    WU = dt_i("WU", [E, 8, D, 128], F32, kind="ExternalInput")
    WD = dt_i("WD", [E, I, D], F32, kind="ExternalInput")
    SG = dt_i("SG", [8, D, 128], F32, kind="ExternalInput")
    SU = dt_i("SU", [8, D, 128], F32, kind="ExternalInput")
    SD = dt_i("SD", [I, D], F32, kind="ExternalInput")

    out = dt_i("out", [T, D], F32, kind="ExternalOutput")
    okind = "ExternalOutput" if debug else "Internal"
    x1_spill = dt_i("x1_spill", [T, D], F32, kind=okind)
    nlb_dram = dt_i("nlb_dram", [NCH, H, C], F32, kind=okind)
    comb_dbg = dt_i("comb_dbg", [128, NT, E], F32,
                    kind="ExternalOutput") if debug else None
    dumpf = int(os.environ.get("KB_DUMP", "0"))
    dumps_s = int(os.environ.get("KB_DUMP_STRIP", "0"))
    if dumpf:
        d_xs = dt_i("d_xs", [128, D], F32, kind="ExternalOutput")
        d_osq = dt_i("d_osq", [128, D], F32, kind="ExternalOutput")
        d_ln = dt_i("d_ln", [128, 8], F32, kind="ExternalOutput")
        d_x2t = dt_i("d_x2t", [128, D], BF16, kind="ExternalOutput")
        d_x2t2 = dt_i("d_x2t2", [128, D], BF16, kind="ExternalOutput")
        d_x2f = dt_i("d_x2f", [128, 8, 256], BF16, kind="ExternalOutput")
        d_kt = dt_i("d_kt", [128, 2, D], BF16, kind="ExternalOutput")
        d_vt = dt_i("d_vt", [128, 2, D], BF16, kind="ExternalOutput")
        d_efm = dt_i("d_efm", [H, 256], F32, kind="ExternalOutput")
        d_nlf = dt_i("d_nlf", [H, 256], F32, kind="ExternalOutput")
    cDump = None

    ctx = ExitStack()
    sb = lambda name, shape, dt: ctx.enter_context(nc.sbuf_tensor(name, shape, dt))
    ps = lambda name: ctx.enter_context(nc.psum_tensor(name, [128, 512], F32))

    p = Plan(nc, ctx)

    dstg = sb("dstg", [128, 8], F32) if dumpf else None
    ident_b = sb("ident_b", [128, 128], BF16)
    ident_f = sb("ident_f", [128, 128], F32)
    eps6 = sb("eps6", [128, 1], F32)
    eps5 = sb("eps5", [128, 1], F32)
    ebias_b = sb("ebias_b", [128, E], F32)
    gw_b = sb("gw_b", [128, 8, E], BF16)
    x3_fm = sb("x3_fm", [128, 8, T], BF16)
    nlb_tm = sb("nlb_tm", [128, NCH, H], F32)
    scores = sb("scores", [128, NT, E], F32)
    sel = sb("sel", [128, NT, E], F32)
    rank = sb("rank", [128, NT, E], F32)
    rtmp = sb("rtmp", [128, NT, E], F32)
    comb = sb("comb", [128, NT, E], F32)
    den = sb("den", [128, NT], F32)

    zoneA = ExitStack()
    sa = lambda name, shape, dt: zoneA.enter_context(nc.sbuf_tensor(name, shape, dt))
    ones16 = sa("ones16", [H, C], F32)
    ndelta = sa("ndelta", [H, 1], F32)
    delta_s = sa("delta_s", [H, 1], F32)
    nw_s = sa("nw_s", [128, 8], F32)
    wq_b = sa("wq_b", [128, 8, D], BF16)
    wk_b = sa("wk_b", [128, 8, D], BF16)
    wv_b = sa("wv_b", [128, 8, D], BF16)
    wo_b = sa("wo_b", [128, 8, D], BF16)
    wf_b = sa("wf_b", [128, 8, H], BF16)
    S_f = sa("S_f", [128, 8, HD], F32)
    S_b = sa("S_b", [128, 8, HD], BF16)
    xs_sl = [sa(f"xs{i}", [128, D], F32) for i in range(2)]
    x2t_sl = [sa(f"x2t{i}", [128, D], BF16) for i in range(2)]
    x2f_sl = [sa(f"x2f{i}", [128, 8, 256], BF16) for i in range(2)]
    qf_sl = [sa(f"qf{i}", [128, 8, 256], BF16) for i in range(2)]
    kf_sl = [sa(f"kf{i}", [128, 8, 256], BF16) for i in range(2)]
    kt_sl = [sa(f"kt{i}", [128, 2, D], BF16) for i in range(2)]
    vt_sl = [sa(f"vt{i}", [128, 2, D], BF16) for i in range(2)]
    nlf_sl = [sa(f"nlf{i}", [H, 256], F32) for i in range(2)]
    efm = sa("efm", [H, 256], F32)
    nlbf_sl = [sa(f"nlbf{i}", [H, C], F32) for i in range(2)]
    dpre_sl = [sa(f"dpre{i}", [128, H, C], F32) for i in range(2)]
    nlast_sl = [sa(f"nlast{i}", [128, H, 1], F32) for i in range(2)]
    kdp = sa("kdp", [128, H], F32)
    kdsc_sl = [sa(f"kdsc{i}", [128, H], F32) for i in range(2)]
    tot_sl = [sa(f"tot{i}", [128, H], F32) for i in range(2)]
    lbex_sl = [sa(f"lbex{i}", [128, H], F32) for i in range(2)]
    kd_sl = [sa(f"kd{i}", [128, H, HD], BF16) for i in range(2)]
    adec_sl = [sa(f"adec{i}", [128, C], BF16) for i in range(4)]
    oacc = sa("oacc", [128, D], F32)
    osq = sa("osq", [128, D], F32)
    ssq = sa("ssq", [128, 1], F32)
    mu = sa("mu", [128, 1], F32)
    sumx = sa("sumx", [128, 1], F32)
    nmusq = sa("nmusq", [128, 1], F32)
    stdv = sa("stdv", [128, 1], F32)
    rstd = sa("rstd", [128, 1], F32)
    ormt = sa("ormt", [128, D], BF16)
    ofm_sl = [sa(f"ofm{i}", [128, 8, C], BF16) for i in range(2)]
    x3t_sl = [sa(f"x3t{i}", [128, D], BF16) for i in range(2)]

    ring_ps = [ps(f"pr{i}") for i in range(4)]
    poA, poB, poC, poD = ps("poA"), ps("poB"), ps("poC"), ps("poD")
    ring = Ring(p, "rg", ring_ps)
    ring_bf = {id(s): s[:].bitcast(BF16) for s in ring_ps}

    r_xs = Ring(p, "r_xs", xs_sl, fill=True)
    r_x2t = Ring(p, "r_x2t", x2t_sl)
    r_x2f = Ring(p, "r_x2f", x2f_sl)
    r_qf = Ring(p, "r_qf", qf_sl)
    r_kf = Ring(p, "r_kf", kf_sl)
    r_kt = Ring(p, "r_kt", kt_sl)
    r_vt = Ring(p, "r_vt", vt_sl)
    r_nlf = Ring(p, "r_nlf", nlf_sl)
    r_nlbf = Ring(p, "r_nlbf", nlbf_sl, fill=True)
    r_dpre = Ring(p, "r_dpre", dpre_sl, fill=True)
    r_nlast = Ring(p, "r_nlast", nlast_sl, fill=True)
    r_kdsc = Ring(p, "r_kdsc", kdsc_sl)
    r_tot = Ring(p, "r_tot", tot_sl)
    r_lb = Ring(p, "r_lb", lbex_sl)
    r_kd = Ring(p, "r_kd", kd_sl)
    r_adec = Ring(p, "r_adec", adec_sl)
    r_ofm = Ring(p, "r_ofm", ofm_sl)
    r_x3t = Ring(p, "r_x3t", x3t_sl)
    r_po = Ring(p, "r_po", [poA, poB])
    r_yp = Ring(p, "r_yp", [(poA, poB), (poC, poD)])

    cTg = Ctr(p, "cTg")
    cSg = Ctr(p, "cSg")
    cVg = Ctr(p, "cVg")
    cGg = Ctr(p, "cGg")
    cSy = Ctr(p, "cSy")
    cVr = Ctr(p, "cVr")
    cVp = Ctr(p, "cVp")
    cVn = Ctr(p, "cVn")
    p.main = {"scalar": cSg, "vector": cVg, "gpsimd": cGg, "sync": cSy}
    cW = Ctr(p, "cW")
    cWs = Ctr(p, "cWs")
    cX = Ctr(p, "cX")
    cNlbW = Ctr(p, "cNlbW")
    cDp = Ctr(p, "cDp")
    cX1s = Ctr(p, "cX1s")
    cX1l = Ctr(p, "cX1l")
    cOut = Ctr(p, "cOut")
    cDump = Ctr(p, "cDump")
    cCst = Ctr(p, "cCst")
    cWo = Ctr(p, "cWo")
    cQe = Ctr(p, "cQe")
    cX3 = Ctr(p, "cX3")
    cCast = Ctr(p, "cCast")
    cCmb = Ctr(p, "cCmb")
    cOev = Ctr(p, "cOev")
    cH = Ctr(p, "cH")

    nopv = lambda v: v.engine_nop()
    nops = lambda sc: sc.nop()
    p.nopf = {"vector": nopv, "scalar": nops, "gpsimd": lambda g: g.nop(),
              "sync": lambda sy: sy.nop(), "tensor": lambda t: t.nop()}

    p.emit("gpsimd", lambda g: g.memset(ident_b[:], 1.0), inc=cGg.inc())
    p.emit("gpsimd", lambda g: g.affine_select(
        out=ident_b[:], in_=ident_b[:], compare_op=ALU.is_equal,
        fill=0.0, base=0, pattern=[[-1, 128]], channel_multiplier=1),
        waits=[cGg.at], inc=cGg.inc())
    p.emit("gpsimd", lambda g: g.memset(ident_f[:], 1.0), inc=cGg.inc())
    p.emit("gpsimd", lambda g: g.affine_select(
        out=ident_f[:], in_=ident_f[:], compare_op=ALU.is_equal,
        fill=0.0, base=0, pattern=[[-1, 128]], channel_multiplier=1),
        waits=[cGg.at], inc=cGg.inc())
    p.emit("gpsimd", lambda g: g.memset(ones16[:], 1.0), inc=cGg.inc())
    p.emit("gpsimd", lambda g: g.memset(eps5[:], 1e-5), inc=cGg.inc())
    p.emit("gpsimd", lambda g: g.memset(eps6[:], 1e-6), waits=[cGg.at],
           inc=cCst.inc())

    for (dst, src, w) in ((wq_b, wqT, D), (wk_b, wkT, D), (wv_b, wvT, D),
                          (wo_b, woT, D), (wf_b, wfT, H), (gw_b, gwT, E)):
        ap_in = bass.AP(src, 0, [[w, 128], [128 * w, 8], [1, w]])
        p.emit("gpsimd", lambda g, d=dst, a=ap_in: g.dma_start(d[:], a),
               inc=cW.inc(16), inc_n=16)
    p.emit("gpsimd", lambda g: g.dma_start(ebias_b[:], ebias[:, :].partition_broadcast(128)),
           inc=cW.inc(16), inc_n=16)
    p.emit("sync", lambda s: s.dma_start(nw_s[:], nw2[:, :]), inc=cWs.inc(16), inc_n=16)
    p.emit("sync", lambda s: s.dma_start(delta_s[:], delta[:, :]), inc=cWs.inc(16), inc_n=16)
    w_all = cW.n
    ws_all = cWs.n

    p.emit("vector", lambda v: v.tensor_scalar_mul(ndelta[:], delta_s[:], -1.0),
           waits=[(cWs.name, ws_all)], inc=cCst.inc())
    for dc in range(8):
        p.emit("vector",
               lambda v, dc=dc: v.tensor_scalar_mul(wo_b[:, dc, :], wo_b[:, dc, :],
                                                    nw_s[:, dc:dc + 1]),
               waits=[(cW.name, w_all), (cWs.name, ws_all)], inc=cWo.inc())
    p.emit("vector", lambda v: v.memset(S_f[:], 0.0), inc=cCst.inc())
    p.emit("vector", lambda v: v.memset(S_b[:], 0.0), inc=cCst.inc())
    consts_n = cCst.n
    wo_n = cWo.n

    def mm_group(pairs, waits):
        def f(t, prs=pairs):
            last = None
            n = len(prs)
            for k, (o, l, r) in enumerate(prs):
                last = t.matmul(o, l, r, start=(k == 0), stop=(k == n - 1))
            return last
        p.emit("tensor", f, waits=waits, inc=cTg.inc())
        return cTg.n

    def transpose4(src_tm, dst_ap_fn, g4, waits, evac_ctr=None, extra_rel=None,
                   copy_extra=()):
        slot, rsem, rw, _ = ring.acquire()
        bfv = ring_bf[id(slot)]
        def tmm(t, bv=bfv, src=src_tm, g4=g4):
            last = None
            for k in range(4):
                dc = g4 * 4 + k
                last = t.matmul(bv[:, k * 128:(k + 1) * 128],
                                src[:, dc * 128:dc * 128 + 128], ident_b[:],
                                start=(k == 0), stop=(k == 3), is_transpose=True)
            return last
        p.emit("tensor", tmm, waits=list(waits) + rw, inc=cTg.inc())
        tv = cTg.n
        ex = list(copy_extra)
        if evac_ctr is not None:
            evac_ctr.n += 1
            ex.append(evac_ctr.name)
        assert len(ex) <= 1, "max 2 sync updates per instruction"
        p.emit("scalar",
               lambda sc, bv=bfv, f=dst_ap_fn: sc.copy(
                   f(), bv[:, 0:512].rearrange("p (a b) -> p a b", a=4)),
               waits=[(cTg.name, tv)], inc=rsem, extra=ex)
        if extra_rel is not None:
            p.emit("scalar", nops, inc=extra_rel)

    n_strips = min(16, level)
    qf = kf = None
    prev_lnread = []
    prev_rms1 = None
    prev_oev = None
    prev_dx = None
    for s in range(n_strips):
        own = (s >= 8) and sub >= 1
        xsrc = x_own if own else x_pre
        base = (s % 8) * 256

        x2f, x2f_sem, x2f_w, _ = r_x2f.acquire()
        for tile in range(2):
            t0 = base + tile * 128
            xsl, xsem, xw, _ = r_xs.acquire(rel_units=2)
            xfs, xfv = r_xs.fill_tok()
            p.emit("sync", lambda sy, xs=xsl, src=xsrc, t0=t0:
                   sy.dma_start(xs[:], src[t0:t0 + 128, :]),
                   waits=xw, inc=xfs, inc_n=16)
            if dumpf and s == dumps_s and tile == 0:
                p.emit("sync", lambda sy, xs=xsl: sy.dma_start(d_xs[:, :], xs[:]),
                       waits=[(xfs, xfv)], inc=cDump.inc(16), inc_n=16)
            x2t, x2tsem, x2tw, _ = r_x2t.acquire(rel_units=2)
            p.emit("scalar", lambda sc, xs=xsl: sc.activation(
                osq[:], xs[:], AF.Square),
                waits=[(xfs, xfv)], inc=xsem, extra=[cSg.inc()])
            sqm = cSg.n
            p.emit("vector", lambda v: v.reduce_sum(ssq[:, 0:1], osq[:], AXX),
                   waits=[(cSg.name, sqm)])
            p.emit("vector", lambda v, xs=xsl: v.reduce_sum(
                sumx[:, 0:1], xs[:], AXX), waits=[(xfs, xfv)], inc=cVr.inc())
            redm = cVr.n
            p.emit("scalar", lambda sc: sc.activation(
                mu[:], sumx[:], AF.Identity, scale=1.0 / D),
                waits=[(cVr.name, redm)] + prev_lnread)
            p.emit("scalar", lambda sc: sc.activation(
                nmusq[:], sumx[:], AF.Square, scale=1.0 / D))
            p.emit("scalar", lambda sc: sc.activation(
                nmusq[:], nmusq[:], AF.Identity, scale=-1.0,
                bias=eps5[:, 0:1]), waits=[(cCst.name, consts_n)],
                inc=cSg.inc())
            nmrdy = cSg.n
            p.emit("scalar", lambda sc: sc.activation(
                stdv[:], ssq[:], AF.Sqrt, bias=nmusq[:, 0:1], scale=1.0 / D),
                waits=[(cSg.name, nmrdy)], inc=cSg.inc())
            sqv = cSg.n
            p.emit("vector", lambda v: v.reciprocal(rstd[:], stdv[:]),
                   waits=[(cSg.name, sqv)], inc=cVp.inc())
            rv = cVp.n
            p.emit("vector", lambda v, xs=xsl, x2=x2t: v.tensor_scalar(
                x2[:], xs[:], mu[:, 0:1], rstd[:, 0:1], ALU.subtract, ALU.mult),
                waits=[(cVp.name, rv)] + x2tw, inc=xsem, extra=[cVg.inc()])
            lnv = cVg.n
            prev_lnread = [(cVg.name, lnv)]
            if dumpf and s == dumps_s and tile == 0:
                p.emit("sync", lambda sy, x2=x2t: sy.dma_start(d_x2t[:, :], x2[:]),
                       waits=[(cVg.name, lnv)], inc=cDump.inc(16), inc_n=16)
                p.emit("sync", lambda sy: sy.dma_start(d_osq[:, :], osq[:]),
                       waits=[(cVg.name, lnv)], inc=cDump.inc(16), inc_n=16)
                for di, dsrc in enumerate((ssq, sumx, mu, nmusq, stdv, rstd,
                                           eps5, eps6)):
                    p.emit("scalar", lambda sc, di=di, dsrc=dsrc:
                           sc.copy(dstg[:, di:di + 1], dsrc[:, 0:1]),
                           waits=[(cVg.name, lnv)],
                           inc=(cSg.inc() if di == 7 else None))
                p.emit("sync", lambda sy: sy.dma_start(d_ln[:, :], dstg[:, :]),
                       waits=[(cSg.name, cSg.n)], inc=cDump.inc(16), inc_n=16)
            for g4 in range(2):
                transpose4(x2t, (lambda x2f=x2f, g4=g4, tile=tile:
                                 x2f[:, g4 * 4:g4 * 4 + 4, tile * 128:tile * 128 + 128]),
                           g4, [(cVg.name, lnv), (cCst.name, consts_n)] +
                           (x2f_w if (tile == 0 and g4 == 0) else []),
                           extra_rel=x2tsem, copy_extra=[cSg.inc()])
        x2rdy = cSg.n
        if dumpf and s == dumps_s:
            p.emit("sync", lambda sy, x2f=x2f: sy.dma_start(d_x2f[:, :, :], x2f[:]),
                   waits=[(cSg.name, x2rdy)], inc=cDump.inc(16), inc_n=16)
            p.emit("sync", lambda sy, x2=x2t: sy.dma_start(d_x2t2[:, :], x2[:]),
                   waits=[(cSg.name, x2rdy)], inc=cDump.inc(16), inc_n=16)

        if own:
            qf, qf_sem, qfw, _ = r_qf.acquire()
            kf, kf_sem, kfw, _ = r_kf.acquire()
            for (dst, dw, wsrc) in ((qf, qfw, wq_b), (kf, kfw, wk_b))[:nqk]:
                for oc in range(8):
                    slot, rsem, rw, _ = ring.acquire()
                    pairs = [(slot[:, 0:256], wsrc[:, dc, oc * 128:oc * 128 + 128],
                              x2f[:, dc, 0:256]) for dc in range(8)]
                    mv = mm_group(pairs, [(cSg.name, x2rdy), (cW.name, w_all)] + rw +
                                  (dw if oc == 0 else []))
                    p.emit("scalar", lambda sc, sl_=slot, d=dst, oc=oc: sc.activation(
                        d[:, oc, :], sl_[:, 0:256], AF.Silu),
                        waits=[(cTg.name, mv)], inc=rsem, extra=[cQe.inc()])
        kt, kt_sem, ktw, _ = r_kt.acquire()
        vt, vt_sem, vtw, _ = r_vt.acquire()
        for (dst, dw, wsrc, act) in ((kt, ktw, wk_b, True), (vt, vtw, wv_b, False)):
            for tc in range(2):
                for oh in range(2):
                    slot, rsem, rw, _ = ring.acquire()
                    pairs = [(slot[:, 0:512], x2f[:, dc, tc * 128:tc * 128 + 128],
                              wsrc[:, dc, oh * 512:oh * 512 + 512]) for dc in range(8)]
                    mv = mm_group(pairs, [(cSg.name, x2rdy), (cW.name, w_all)] + rw +
                                  (dw if (tc == 0 and oh == 0) else []))
                    fn = AF.Silu if act else AF.Identity
                    p.emit("scalar", lambda sc, sl_=slot, d=dst, tc=tc, oh=oh, fn=fn:
                           sc.activation(d[:, tc, oh * 512:oh * 512 + 512],
                                         sl_[:, 0:512], fn),
                           waits=[(cTg.name, mv)], inc=rsem, extra=[cQe.inc()])
        if dumpf and s == dumps_s:
            p.emit("sync", lambda sy, kt=kt: sy.dma_start(d_kt[:, :, :], kt[:]),
                   waits=[(cQe.name, cQe.n)], inc=cDump.inc(16), inc_n=16)
            p.emit("sync", lambda sy, vt=vt: sy.dma_start(d_vt[:, :, :], vt[:]),
                   waits=[(cQe.name, cQe.n)], inc=cDump.inc(16), inc_n=16)
        nlf, nlf_sem, nlfw, _ = r_nlf.acquire()
        slot, rsem, rw, _ = ring.acquire()
        pairs = [(slot[0:16, 0:256], wf_b[:, dc, :], x2f[:, dc, 0:256])
                 for dc in range(8)]
        fmm = mm_group(pairs, [(cSg.name, x2rdy), (cW.name, w_all)] + rw + nlfw)
        p.emit("scalar", lambda sc, sl_=slot: sc.activation(
            efm[:], sl_[0:16, 0:256], AF.Exp, bias=ndelta[:, 0:1], scale=-1.0),
            waits=[(cTg.name, fmm), (cCst.name, consts_n)], inc=rsem)
        p.emit("scalar", lambda sc, d=nlf: sc.activation(d[:], efm[:], AF.Ln, bias=1.0),
               inc=cSg.inc())
        fsem = cSg.n
        if dumpf and s == dumps_s:
            p.emit("sync", lambda sy: sy.dma_start(d_efm[:, :], efm[:]),
                   waits=[(cSg.name, fsem)], inc=cDump.inc(16), inc_n=16)
            p.emit("sync", lambda sy, nlf=nlf: sy.dma_start(d_nlf[:, :], nlf[:]),
                   waits=[(cSg.name, fsem)], inc=cDump.inc(16), inc_n=16)
        p.emit("vector", nopv, waits=[(cTg.name, fmm)], inc=x2f_sem)
        qeN = cQe.n

        for cc in range(2):
            j = s * 2 + cc
            own_j = j >= 16
            nlbf, nlbf_sem, nlbfw, _ = r_nlbf.acquire()
            p.emit("vector", lambda v, d=nlbf, src=nlf, cc=cc:
                   v.tensor_tensor_scan(d[:], ones16[:], src[:, cc * 128:cc * 128 + 128],
                                        0.0, ALU.mult, ALU.add),
                   waits=[(cSg.name, fsem), (cCst.name, consts_n)] + nlbfw,
                   inc=cVn.inc())
            nlbv = cVn.n
            slot2, rsem2, rw2, _ = ring.acquire()
            p.emit("tensor", lambda t, sl_=slot2, src=nlbf:
                   t.matmul(sl_[0:128, 0:16], src[:], ident_f[0:16, 0:16],
                            start=True, stop=True, is_transpose=True),
                   waits=[(cVn.name, nlbv), (cCst.name, consts_n)] + rw2,
                   inc=cTg.inc())
            tv2 = cTg.n
            p.emit("vector", lambda v, sl_=slot2, j=j:
                   v.tensor_copy(nlb_tm[:, j, :], sl_[0:128, 0:16]),
                   waits=[(cTg.name, tv2)], inc=rsem2, extra=[cVg.inc()])
            ntv = cVg.n
            nbfs, nbfv = r_nlbf.fill_tok()
            p.emit("sync", lambda sy, src=nlbf, j=j:
                   sy.dma_start(nlb_dram[j, :, :], src[:]),
                   waits=[(cVn.name, nlbv)], inc=nbfs, inc_n=16)
            p.emit("vector", nopv, waits=[(nbfs, nbfv)], inc=nlbf_sem)

            tot, tot_sem, totw, _ = r_tot.acquire()
            if own_j:
                dpre, dpre_sem, dprew, _ = r_dpre.acquire()
                dfs, dfv = r_dpre.fill_tok()
                p.emit("gpsimd", lambda g, d=dpre, j=j:
                       g.dma_start(d[:], nlb_dram[j, :, :].partition_broadcast(128)),
                       waits=[(nbfs, nbfv)] + dprew, inc=dfs, inc_n=16)
                p.emit("gpsimd", lambda g, d=dpre:
                       g.affine_select(out=d[:], in_=d[:], compare_op=ALU.is_ge,
                                       fill=1.0e30, base=0,
                                       pattern=[[0, H], [1, C]], channel_multiplier=-1),
                       waits=[(dfs, dfv)], inc=cGg.inc())
                mskv = cGg.n
                p.emit("scalar", lambda sc, d=dpre, dst=tot:
                       sc.activation(dst[:], d[:, :, 127], AF.Exp, scale=-1.0),
                       waits=[(cGg.name, mskv)] + totw, inc=cSg.inc())
                for h in range(H):
                    p.emit("scalar", lambda sc, d=dpre, h=h, j=j:
                           sc.activation(d[:, h, :], d[:, h, :], AF.Exp,
                                         bias=nlb_tm[:, j, h:h + 1], scale=-1.0),
                           waits=([(cVg.name, ntv)] if h == 0 else []),
                           inc=(cSg.inc() if h == H - 1 else None))
                dxv = cSg.n
                totv = dxv
                kdsc_col = lambda h, d=dpre: d[:, h, 127:128]
            else:
                nlast, nlast_sem, nlastw, _ = r_nlast.acquire(rel_units=2)
                nfs, nfv = r_nlast.fill_tok()
                p.emit("gpsimd", lambda g, d=nlast, j=j:
                       g.dma_start(d[:], nlb_dram[j, :, 127:128].partition_broadcast(128),
                                   allow_slow_non_contiguous=True),
                       waits=[(nbfs, nbfv)] + nlastw, inc=nfs, inc_n=16)
                kdsc, kdsc_sem, kdscw, _ = r_kdsc.acquire()
                kdp_w = [(nfs, nfv), (cVg.name, ntv)]
                if prev_dx is not None:
                    kdp_w.append(prev_dx)
                p.emit("vector", lambda v, nl=nlast, j=j:
                       v.tensor_sub(kdp[:], nlb_tm[:, j, :], nl[:, :, 0]),
                       waits=kdp_w, inc=nlast_sem, extra=[cVg.inc()])
                kdpv = cVg.n
                p.emit("scalar", lambda sc, d=kdsc:
                       sc.activation(d[:], kdp[:], AF.Exp),
                       waits=[(cVg.name, kdpv)] + kdscw, inc=cSg.inc())
                dxv = cSg.n
                prev_dx = (cSg.name, dxv)
                p.emit("scalar", lambda sc, nl=nlast, dst=tot:
                       sc.activation(dst[:], nl[:, :, 0], AF.Exp, scale=-1.0),
                       waits=[(nfs, nfv)] + totw, inc=cSg.inc(),
                       extra=[nlast_sem])
                totv = cSg.n
                kdsc_col = lambda h, d=kdsc: d[:, h:h + 1]

            if own_j and sub >= 2:
                tj = j - 16
                cast_need = j - 15
                bkA, psA, pwA, _ = r_po.acquire()
                bkB, psB, pwB, _ = r_po.acquire()
                def ointf(t, bA=bkA, bB=bkB, qf=qf, cc=cc):
                    last = None
                    for i in range(8):
                        for par, bank in ((0, bA), (1, bB)):
                            h = 2 * i + par
                            pb = par * 64
                            last = t.matmul(
                                bank[:, i * 64:i * 64 + 64],
                                qf[pb:pb + 64, h // 2, cc * 128:cc * 128 + 128],
                                S_b[pb:pb + 64, h // 2, :],
                                start=(i == 0), stop=(i == 7))
                    return last
                p.emit("tensor", ointf,
                       waits=[(cCast.name, cast_need), (cQe.name, qeN)] + pwA + pwB,
                       inc=cTg.inc())
                ointv = cTg.n
                lb, lb_sem, lbw, _ = r_lb.acquire(rel_units=2)
                p.emit("scalar", lambda sc, j=j, lb=lb: sc.activation(
                    lb[:], nlb_tm[:, j, :], AF.Exp, scale=-1.0),
                    waits=[(cVg.name, ntv)] + lbw, inc=cSg.inc())
                lbv = cSg.n
                for bk, bank, psem in ((0, bkA, psA), (1, bkB, psB)):
                    evw = [(cTg.name, ointv), (cSg.name, lbv)]
                    if bk == 0 and prev_rms1 is not None:
                        evw.append(prev_rms1)
                    oacc_st = bass.AP(oacc, bk * 64, [[1024, 128], [128, 8], [1, 64]])
                    lb_st = bass.AP(lb, bk, [[16, 128], [2, 8], [0, 64]])
                    p.emit("vector", lambda v, bank=bank, o_=oacc_st, l_=lb_st:
                           v.tensor_tensor(
                               o_, bank[:, 0:512].rearrange("p (a b) -> p a b", a=8),
                               l_, ALU.mult),
                           waits=evw, inc=psem, extra=[lb_sem])
                bkA2, psA2, pwA2, _ = (None, None, None, None)
                bkB2, psB2, pwB2, _ = (None, None, None, None)
                if sub >= 4:
                    bkA2, psA2, pwA2, _ = r_po.acquire()
                    bkB2, psB2, pwB2, _ = r_po.acquire()

            kd, kd_sem, kdw, _ = r_kd.acquire()
            for h in range(H):
                p.emit("vector", lambda v, kd=kd, kt=kt, cc=cc, h=h, col=kdsc_col:
                       v.tensor_scalar_mul(kd[:, h, :], kt[:, cc, h * 64:h * 64 + 64],
                                           col(h)),
                       waits=[(cSg.name, dxv), (cQe.name, qeN)] +
                             (kdw if h == 0 else []),
                       inc=(cVg.inc() if h == H - 1 else None))
            kdv = cVg.n

            oclA = oclB = None
            for h in range(H):
                pb = (h % 2) * 64
                if own_j and sub >= 3:
                    slotA, rsemA, rwA, adn = ring.acquire()
                    p.emit("tensor", lambda t, sl_=slotA, kf=kf, qf=qf, cc=cc, pb=pb, h=h:
                           t.matmul(sl_[:, 0:128],
                                    kf[pb:pb + 64, h // 2, cc * 128:cc * 128 + 128],
                                    qf[pb:pb + 64, h // 2, cc * 128:cc * 128 + 128],
                                    start=True, stop=True),
                           waits=[(cQe.name, qeN)] + rwA, inc=cTg.inc())
                    av = cTg.n
                    ad, adsem, adw, _ = r_adec.acquire()
                    p.emit("vector", lambda v, sl_=slotA, ad=ad, d=dpre, h=h:
                           v.tensor_mul(ad[:], sl_[:, 0:128], d[:, h, :]),
                           waits=[(cTg.name, av), (cSg.name, dxv)] + adw, inc=rsemA)
                    if sub >= 4:
                        bank = bkA2 if h % 2 == 0 else bkB2
                        i = h // 2
                        oi_w = [(rsemA, adn)]
                        if h == 0:
                            oi_w += pwA2
                        if h == 1:
                            oi_w += pwB2
                        p.emit("tensor", lambda t, bank=bank, ad=ad, vt=vt, cc=cc, i=i, h=h:
                               t.matmul(bank[:, i * 64:i * 64 + 64], ad[:],
                                        vt[:, cc, h * 64:h * 64 + 64],
                                        start=(i == 0), stop=(i == 7)),
                               waits=oi_w, inc=(adsem if h not in (14, 15) else cTg.inc()))
                        if h == 14:
                            oclA = cTg.n
                            p.emit("vector", nopv, waits=[(cTg.name, oclA)], inc=adsem)
                        elif h == 15:
                            oclB = cTg.n
                            p.emit("vector", nopv, waits=[(cTg.name, oclB)], inc=adsem)
                    else:
                        p.emit("vector", nopv, waits=[(rsemA, adn)], inc=adsem)
                slotS, rsemS, rwS, _ = ring.acquire()
                tp = (0, 64) if (h % 2) else None
                def supf(t, sl_=slotS, kd=kd, vt=vt, cc=cc, h=h, pb=pb, tp=tp):
                    kw = dict(start=True, stop=True)
                    if tp is not None:
                        kw["tile_position"] = tp
                    return t.matmul(sl_[pb:pb + 64, 0:64], kd[:, h, :],
                                    vt[:, cc, h * 64:h * 64 + 64], **kw)
                p.emit("tensor", supf,
                       waits=[(cVg.name, kdv), (cQe.name, qeN)] + rwS, inc=cTg.inc())
                sv = cTg.n
                p.emit("vector", lambda v, sl_=slotS, h=h, pb=pb, tot=tot:
                       v.scalar_tensor_tensor(S_f[pb:pb + 64, h // 2, :],
                                              S_f[pb:pb + 64, h // 2, :],
                                              tot[pb:pb + 64, h:h + 1],
                                              sl_[pb:pb + 64, 0:64],
                                              ALU.mult, ALU.add),
                       waits=[(cTg.name, sv), (cSg.name, totv)], inc=rsemS,
                       extra=([cVg.inc()] if h == H - 1 else []))

            mrgv = cVg.n
            tloop_n = cTg.n
            p.emit("vector", nopv, waits=[(cVg.name, mrgv)], inc=tot_sem)
            p.emit("vector", nopv, waits=[(cTg.name, tloop_n)], inc=kd_sem)
            if own_j:
                p.emit("vector", nopv,
                       waits=[(cSg.name, dxv), (cVg.name, mrgv),
                              (cTg.name, tloop_n)], inc=dpre_sem)
            else:
                p.emit("vector", nopv, waits=[(cVg.name, kdv)], inc=kdsc_sem)
            if j >= 15:
                wts = [(cTg.name, ointv)] if (own_j and sub >= 2) else []
                wts = wts + [(cVg.name, mrgv)]
                for h in range(H):
                    pb = (h % 2) * 64
                    p.emit("vector", lambda v, h=h, pb=pb:
                           v.tensor_copy(S_b[pb:pb + 64, h // 2, :],
                                         S_f[pb:pb + 64, h // 2, :]),
                           waits=(wts if h == 0 else []),
                           inc=(cCast.inc() if h == H - 1 else None))

            if own_j and sub >= 4 and sub < 5:
                p.emit("vector", nopv, waits=[(cTg.name, oclA)], inc=psA2)
                p.emit("vector", nopv, waits=[(cTg.name, oclB)], inc=psB2)
            if own_j and sub >= 5:
                oacc_e = bass.AP(oacc, 0, [[1024, 128], [128, 8], [1, 64]])
                oacc_o = bass.AP(oacc, 64, [[1024, 128], [128, 8], [1, 64]])
                p.emit("vector", lambda v, bank=bkA2, o_=oacc_e: v.scalar_tensor_tensor(
                    o_, bank[:, 0:512].rearrange("p (a b) -> p a b", a=8), 1.0, o_,
                    ALU.mult, ALU.add), waits=[(cTg.name, oclA)], inc=psA2)
                p.emit("vector", lambda v, bank=bkB2, o_=oacc_o: v.scalar_tensor_tensor(
                    o_, bank[:, 0:512].rearrange("p (a b) -> p a b", a=8), 1.0, o_,
                    ALU.mult, ALU.add), waits=[(cTg.name, oclB)], inc=psB2,
                    extra=[cVg.inc()])
                ofs = cVg.n
                p.emit("scalar", lambda sc: sc.activation(
                    osq[:], oacc[:], AF.Square),
                    waits=[(cVg.name, ofs)], inc=cSg.inc())
                prev_rms1 = (cSg.name, cSg.n)
                p.emit("vector", lambda v: v.reduce_sum(ssq[:, 0:1], osq[:], AXX),
                       waits=[prev_rms1], inc=cVr.inc())
                rmsred = cVr.n
                p.emit("scalar", lambda sc: sc.activation(
                    stdv[:], ssq[:], AF.Sqrt, bias=eps6[:, 0:1], scale=1.0 / D),
                    waits=[(cCst.name, consts_n), (cVr.name, rmsred)] +
                          prev_lnread,
                    inc=cSg.inc())
                rms2 = cSg.n
                p.emit("vector", lambda v: v.reciprocal(rstd[:], stdv[:]),
                       waits=[(cSg.name, rms2)], inc=cVp.inc())
                rv2 = cVp.n
                orm_w = [prev_oev] if prev_oev is not None else []
                p.emit("vector", lambda v: v.tensor_scalar_mul(ormt[:], oacc[:],
                                                               rstd[:, 0:1]),
                       waits=[(cVp.name, rv2)] + orm_w, inc=cVg.inc())
                ormv = cVg.n
                prev_lnread = [(cVg.name, ormv)]
                ofm, ofm_sem, ofmw, _ = r_ofm.acquire()
                for g4 in range(2):
                    transpose4(ormt, (lambda ofm=ofm, g4=g4:
                                      ofm[:, g4 * 4:g4 * 4 + 4, :]),
                               g4, [(cVg.name, ormv)] + (ofmw if g4 == 0 else []),
                               evac_ctr=cOev)
                prev_oev = (cOev.name, cOev.n)
                xre, xre_sem, xrew, _ = r_xs.acquire(rel_units=3)
                rfs, rfv = r_xs.fill_tok()
                p.emit("sync", lambda sy, xr=xre, tj=tj:
                       sy.dma_start(xr[:], x_own[tj * 128:tj * 128 + 128, :]),
                       waits=xrew, inc=rfs, inc_n=16)
                x1s = xre
                for oh in range(2):
                    slot3, rsem3, rw3, _ = ring.acquire()
                    pairs = [(slot3[:, 0:512], ofm[:, dc, :],
                              wo_b[:, dc, oh * 512:oh * 512 + 512]) for dc in range(8)]
                    mv3 = mm_group(pairs, [(cOev.name, cOev.n), (cWo.name, wo_n)] + rw3)
                    p.emit("vector", lambda v, sl_=slot3, x1=x1s, oh=oh:
                           v.tensor_add(x1[:, oh * 512:oh * 512 + 512], sl_[:, 0:512],
                                        x1[:, oh * 512:oh * 512 + 512]),
                           waits=[(cTg.name, mv3), (rfs, rfv)], inc=rsem3,
                           extra=([cVg.inc()] if oh == 1 else []))
                    if oh == 1:
                        p.emit("vector", nopv, inc=ofm_sem)
                x1rdy = cVg.n
                p.emit("sync", lambda sy, x1=x1s, tj=tj:
                       sy.dma_start(x1_spill[tj * 128:tj * 128 + 128, :], x1[:]),
                       waits=[(cVg.name, x1rdy)], inc=cX1s.inc(16), inc_n=16)
                x1sv = cX1s.n
                p.emit("vector", nopv, waits=[(cX1s.name, x1sv)], inc=xre_sem)

                x3t, x3tsem, x3tw, _ = r_x3t.acquire(rel_units=2)
                p.emit("scalar", lambda sc, x1=x1s: sc.activation(
                    osq[:], x1[:], AF.Square),
                    waits=[(cVg.name, x1rdy)], inc=xre_sem, extra=[cSg.inc()])
                sqm2 = cSg.n
                p.emit("vector", lambda v: v.reduce_sum(ssq[:, 0:1], osq[:], AXX),
                       waits=[(cSg.name, sqm2)])
                p.emit("vector", lambda v, x1=x1s: v.reduce_sum(
                    sumx[:, 0:1], x1[:], AXX), waits=[(cVg.name, x1rdy)],
                    inc=cVr.inc())
                redm2 = cVr.n
                p.emit("scalar", lambda sc: sc.activation(
                    mu[:], sumx[:], AF.Identity, scale=1.0 / D),
                    waits=[(cVr.name, redm2)] + prev_lnread)
                p.emit("scalar", lambda sc: sc.activation(
                    nmusq[:], sumx[:], AF.Square, scale=1.0 / D))
                p.emit("scalar", lambda sc: sc.activation(
                    nmusq[:], nmusq[:], AF.Identity, scale=-1.0,
                    bias=eps5[:, 0:1]), waits=[(cCst.name, consts_n)],
                    inc=cSg.inc())
                nmrdy2 = cSg.n
                p.emit("scalar", lambda sc: sc.activation(
                    stdv[:], ssq[:], AF.Sqrt, bias=nmusq[:, 0:1], scale=1.0 / D),
                    waits=[(cSg.name, nmrdy2)], inc=cSg.inc())
                sq2 = cSg.n
                p.emit("vector", lambda v: v.reciprocal(rstd[:], stdv[:]),
                       waits=[(cSg.name, sq2)], inc=cVp.inc())
                rv3 = cVp.n
                p.emit("vector", lambda v, x1=x1s, x3=x3t: v.tensor_scalar(
                    x3[:], x1[:], mu[:, 0:1], rstd[:, 0:1], ALU.subtract, ALU.mult),
                    waits=[(cVp.name, rv3)] + x3tw, inc=xre_sem,
                    extra=[cVg.inc()])
                ln2v = cVg.n
                prev_lnread = [(cVg.name, ln2v)]
                for g4 in range(2):
                    transpose4(x3t, (lambda g4=g4, tj=tj:
                                     x3_fm[:, g4 * 4:g4 * 4 + 4, tj * 128:tj * 128 + 128]),
                               g4, [(cVg.name, ln2v)],
                               evac_ctr=(cX3 if g4 == 1 else None),
                               extra_rel=x3tsem)

        mgn = [(cTg.name, cTg.n), (cVg.name, cVg.n), (cSg.name, cSg.n),
               (cVn.name, cVn.n)]
        if own:
            p.emit("vector", nopv, waits=mgn, inc=qf_sem)
            p.emit("vector", nopv, waits=mgn, inc=kf_sem)
        p.emit("vector", nopv, waits=mgn, inc=kt_sem)
        p.emit("vector", nopv, waits=mgn, inc=vt_sem)
        p.emit("vector", nopv, waits=mgn, inc=nlf_sem)

    if level < 17:
        fin = [(n, t) for n, t in p.totals.items() if t > 0]
        p.emit("sync", lambda sy: sy.nop(), waits=fin)
        with nc.Block() as block:
            @block.tensor
            def _(t):
                p.run("tensor", t)

            @block.vector
            def _(v):
                p.run("vector", v)

            @block.scalar
            def _(sc):
                p.run("scalar", sc)

            @block.gpsimd
            def _(g):
                p.run("gpsimd", g)

            @block.sync
            def _(sy):
                p.run("sync", sy)
        zoneA.close()
        ctx.close()
        return nc

    attn_done = [(cX3.name, 16)]

    zoneA.close()
    x1_all = sb("x1_all", [128, NT, D], F32)
    h_sb = sb("h_sb", [128, 8, T], BF16)
    wd_sl = [sb(f"wd{i}", [128, 8, D], BF16) for i in range(2)]
    wg_sl = [sb(f"wg{i}", [128, 8, 128], BF16) for i in range(2)]
    wu_sl = [sb(f"wu{i}", [128, 8, 128], BF16) for i in range(2)]
    sg_sl = [sb(f"sg{i}", [128, 512], F32) for i in range(2)]
    r_wd = Ring(p, "r_wd", wd_sl, fill=True)
    r_wg = Ring(p, "r_wg", wg_sl, fill=True)
    r_wu = Ring(p, "r_wu", wu_sl, fill=True)
    r_sg = Ring(p, "r_sg", sg_sl)

    for tj in range(NT):
        p.emit("sync", lambda sy, tj=tj:
               sy.dma_start(x1_all[:, tj, :], x1_spill[tj * 128:tj * 128 + 128, :]),
               waits=[(cX1s.name, 16 * NT)] + (attn_done if tj == 0 else []),
               inc=cX1l.inc(16), inc_n=16)

    for tj in range(NT):
        slot, rsem, rw, _ = ring.acquire()
        pairs = [(slot[:, 0:16], x3_fm[:, dc, tj * 128:tj * 128 + 128], gw_b[:, dc, :])
                 for dc in range(8)]
        mv = mm_group(pairs, attn_done + [(cW.name, w_all)] + rw)
        p.emit("scalar", lambda sc, sl_=slot, tj=tj: sc.activation(
            scores[:, tj, :], sl_[:, 0:16], AF.Exp, scale=-1.0),
            waits=[(cTg.name, mv)], inc=rsem, extra=[cSg.inc()])
    gt_done = cSg.n

    vlast = [None]

    def vch(fn, ctr, extra=()):
        w = list(extra)
        if vlast[0] is not None:
            w.append(vlast[0])
        p.emit("vector", fn, waits=w, inc=ctr.inc())
        vlast[0] = (ctr.name, ctr.n)
    vch(lambda v: v.tensor_scalar(
        scores[:], scores[:], 1.0, None, ALU.add), cVg, [(cSg.name, gt_done)])
    vch(lambda v: v.reciprocal(scores[:], scores[:]), cVp)
    vch(lambda v: v.tensor_tensor(
        sel[:], scores[:], ebias_b[:].unsqueeze(1).broadcast_to([128, NT, E]),
        ALU.add), cVg, [(cW.name, w_all)])
    vch(lambda v: v.memset(rank[:], 0.0), cVg)
    for sft in range(1, 16):
        vch(lambda v, sf=sft: v.tensor_tensor(
            rtmp[:, :, sf:], sel[:, :, 0:E - sf], sel[:, :, sf:], ALU.is_ge), cVg)
        vch(lambda v, sf=sft: v.tensor_add(
            rank[:, :, sf:], rank[:, :, sf:], rtmp[:, :, sf:]), cVg)
        vch(lambda v, sf=sft: v.tensor_tensor(
            rtmp[:, :, 0:E - sf], sel[:, :, sf:], sel[:, :, 0:E - sf], ALU.is_gt),
            cVg)
        vch(lambda v, sf=sft: v.tensor_add(
            rank[:, :, 0:E - sf], rank[:, :, 0:E - sf], rtmp[:, :, 0:E - sf]), cVg)
    vch(lambda v: v.tensor_single_scalar(rank[:], rank[:], 8.0, ALU.is_lt), cVg)
    vch(lambda v: v.tensor_mul(comb[:], scores[:], rank[:]), cVg)
    vch(lambda v: v.tensor_reduce(
        den[:].unsqueeze(2), comb[:], AXX, ALU.add), cVr)
    vch(lambda v: v.reciprocal(den[:], den[:]), cVp)
    vch(lambda v: v.tensor_scalar_mul(den[:], den[:], RS), cVg)
    p.emit("vector", lambda v: v.tensor_tensor(
        comb[:], comb[:], den[:].unsqueeze(2).broadcast_to([128, NT, E]), ALU.mult),
        waits=[vlast[0]], inc=cCmb.inc())
    if debug:
        p.emit("sync", lambda sy: sy.dma_start(comb_dbg[:, :, :], comb[:]),
               waits=[(cCmb.name, 1)], inc=cOut.inc(16), inc_n=16)

    cFl = Ctr(p, "cFl")
    n_exp = (E + 1) if level >= 99 else max(0, level - 17)
    if level < 99:
        pass
    for e in range(n_exp) if level < 99 else range(E + 1):
        wdsl, wdsem, wdw, _ = r_wd.acquire()
        off_wd = 0 if e == E else e * I * D
        src_wd = SD if e == E else WD
        wdfs, wdfv = r_wd.fill_tok()
        p.emit("gpsimd", lambda g, d=wdsl, src=src_wd, off=off_wd:
               g.dma_start(d[:], bass.AP(src, off, [[D, 128], [128 * D, 8], [1, D]])),
               waits=wdw + (attn_done if e == 0 else []),
               inc=wdfs, inc_n=16)
        for ib in range(8):
            wgsl, wgsem, wgw, _ = r_wg.acquire()
            wusl, wusem, wuw, _ = r_wu.acquire()
            off = (e * 8 + ib) * D * 128 if e < E else ib * D * 128
            src_g = SG if e == E else WG
            src_u = SU if e == E else WU
            wgfs, wgfv = r_wg.fill_tok()
            p.emit("gpsimd", lambda g, d=wgsl, src=src_g, off=off:
                   g.dma_start(d[:], bass.AP(src, off,
                                             [[128, 128], [128 * 128, 8], [1, 128]])),
                   waits=wgw + (attn_done if (e == 0 and ib == 0) else []),
                   inc=wgfs, inc_n=16)
            wufs, wufv = r_wu.fill_tok()
            p.emit("gpsimd", lambda g, d=wusl, src=src_u, off=off:
                   g.dma_start(d[:], bass.AP(src, off,
                                             [[128, 128], [128 * 128, 8], [1, 128]])),
                   waits=wuw, inc=wufs, inc_n=16)
            for tt in range(4):
                hgs, hgsem, hgw, _ = ring.acquire()
                hus, husem, huw, _ = ring.acquire()
                pairs = [(hgs[:, 0:512], wgsl[:, dc, :],
                          x3_fm[:, dc, tt * 512:tt * 512 + 512]) for dc in range(8)]
                mg = mm_group(pairs, [(wgfs, wgfv)] + hgw + attn_done)
                pairs = [(hus[:, 0:512], wusl[:, dc, :],
                          x3_fm[:, dc, tt * 512:tt * 512 + 512]) for dc in range(8)]
                mu_ = mm_group(pairs, [(wufs, wufv)] + huw)
                sgs, sgsem, sgw, _ = r_sg.acquire()
                p.emit("scalar", lambda sc, hg=hgs, sg=sgs:
                       sc.activation(sg[:], hg[:, 0:512], AF.Silu),
                       waits=[(cTg.name, mg)] + sgw, inc=hgsem,
                       extra=[cSg.inc()])
                ssil = cSg.n
                p.emit("vector", lambda v, hu=hus, sg=sgs, ib=ib, tt=tt:
                       v.tensor_mul(h_sb[:, ib, tt * 512:tt * 512 + 512],
                                    sg[:], hu[:, 0:512]),
                       waits=[(cSg.name, ssil), (cTg.name, mu_)], inc=husem,
                       extra=[cH.inc()])
                p.emit("vector", nopv, waits=[(cH.name, cH.n)], inc=sgsem)
            p.emit("vector", nopv, waits=[(cH.name, cH.n)], inc=wgsem)
            p.emit("vector", nopv, waits=[(cH.name, cH.n)], inc=wusem)
        last_y = None
        for tj in range(NT):
            tt = tj // 4
            hneed = e * 32 + 28 + tt + 1
            yp, ysm, yw, _ = r_yp.acquire(rel_units=2)
            def yf(t, yp=yp, tj=tj, wd=wdsl):
                last = None
                for dh in range(2):
                    bank = yp[dh]
                    for ic in range(8):
                        last = t.matmul(bank[:, 0:512],
                                        h_sb[:, ic, tj * 128:tj * 128 + 128],
                                        wd[:, ic, 512 * dh:512 * dh + 512],
                                        start=(ic == 0), stop=(ic == 7))
                return last
            p.emit("tensor", yf,
                   waits=[(cH.name, hneed), (wdfs, wdfv)] + yw, inc=cTg.inc())
            ygv = cTg.n
            last_y = ygv
            scal = 1.0 if e == E else comb[:, tj, e:e + 1]
            for dh in range(2):
                p.emit("vector", lambda v, yp=yp, tj=tj, dh=dh, sc_=scal:
                       v.scalar_tensor_tensor(x1_all[:, tj, dh * 512:dh * 512 + 512],
                                              yp[dh][:, 0:512], sc_,
                                              x1_all[:, tj, dh * 512:dh * 512 + 512],
                                              ALU.mult, ALU.add),
                       waits=[(cTg.name, ygv), (cCmb.name, 1),
                              (cX1l.name, 16 * NT)],
                       inc=ysm,
                       extra=([cFl.inc()] if (e == E and dh == 1) else []))
            if e == E:
                p.emit("sync", lambda sy, tj=tj:
                       sy.dma_start(out[tj * 128:tj * 128 + 128, :], x1_all[:, tj, :]),
                       waits=[(cFl.name, cFl.n)], inc=cOut.inc(16), inc_n=16)
        p.emit("vector", nopv, waits=[(cTg.name, last_y)], inc=wdsem)

    if level >= 99:
        p.emit("sync", lambda sy: sy.nop(), waits=[(cOut.name, cOut.n)])
    else:
        fin = [(n, t) for n, t in p.totals.items() if t > 0]
        p.emit("sync", lambda sy: sy.nop(), waits=fin)
    n_sems = len(p.sems)
    assert n_sems <= 250, f"too many semaphores: {n_sems}"

    with nc.Block() as block:
        @block.tensor
        def _(t):
            p.run("tensor", t)

        @block.vector
        def _(v):
            p.run("vector", v)

        @block.scalar
        def _(sc):
            p.run("scalar", sc)

        @block.gpsimd
        def _(g):
            p.run("gpsimd", g)

        @block.sync
        def _(sy):
            p.run("sync", sy)

    ctx.close()
    return nc


def _prep_inputs(L0_attn_x, L0_attn_Wq, L0_attn_Wk, L0_attn_Wv, L0_attn_Wo,
                 L0_attn_Wf, L0_attn_delta, L0_attn_norm_weight,
                 L0_ffn_gate_w, L0_ffn_expert_bias, L0_ffn_w_gate, L0_ffn_w_up,
                 L0_ffn_w_down, L0_ffn_shared_gate_w, L0_ffn_shared_up_w,
                 L0_ffn_shared_down_w):
    c = np.ascontiguousarray

    def f(a):
        return np.asarray(a, dtype=np.float32)

    x = f(L0_attn_x)
    shared = {
        "wqT": c(f(L0_attn_Wq).T),
        "wkT": c(f(L0_attn_Wk).T),
        "wvT": c(f(L0_attn_Wv).T),
        "woT": c(f(L0_attn_Wo).T),
        "wfT": c(f(L0_attn_Wf).T),
        "delta": c(f(L0_attn_delta).reshape(H, 1)),
        "nw2": c(f(L0_attn_norm_weight).reshape(8, 128).T),
        "gwT": c(f(L0_ffn_gate_w).T),
        "ebias": c(f(L0_ffn_expert_bias).reshape(1, E)),
        "WG": c(f(L0_ffn_w_gate).reshape(E, 8, 128, D).transpose(0, 1, 3, 2)),
        "WU": c(f(L0_ffn_w_up).reshape(E, 8, 128, D).transpose(0, 1, 3, 2)),
        "WD": c(f(L0_ffn_w_down).transpose(0, 2, 1)),
        "SG": c(f(L0_ffn_shared_gate_w).reshape(8, 128, D).transpose(0, 2, 1)),
        "SU": c(f(L0_ffn_shared_up_w).reshape(8, 128, D).transpose(0, 2, 1)),
        "SD": c(f(L0_ffn_shared_down_w).T),
    }
    zeros = np.zeros((T, D), np.float32)
    in_maps = []
    for core in range(8):
        b, half = core // 2, core % 2
        m = dict(shared)
        m["x_own"] = c(x[b, half * T:(half + 1) * T, :])
        m["x_pre"] = c(x[b, 0:T, :]) if half == 1 else zeros
        in_maps.append(m)
    return in_maps


def kernel(**inputs):
    in_maps = _prep_inputs(**inputs)
    nc = build_nc()
    res = run_bass_kernel_spmd(nc, in_maps, core_ids=list(range(8)))
    outf = np.empty((B, N, D), np.float32)
    for core in range(8):
        b, half = core // 2, core % 2
        outf[b, half * T:(half + 1) * T, :] = res.results[core]["out"]
    return outf
